# version 03
# baseline (speedup 1.0000x reference)
import contextlib
import numpy as np

import concourse.bass as bass
import concourse.mybir as mybir
from concourse.bass_utils import run_bass_kernel_spmd

F32 = mybir.dt.float32
BF16 = mybir.dt.bfloat16
I32 = mybir.dt.int32
U16 = mybir.dt.uint16
AF = mybir.ActivationFunctionType
ALU = mybir.AluOpType
AX = mybir.AxisListType

B, N, D, H, HD = 2, 16384, 2048, 16, 128
NCORES, SPANS = 8, 4
SPAN = 4096
HALO = 2048
WIN = HALO + SPAN
CH = 128
NCH = WIN // CH
HCH = HALO // CH
OCH = SPAN // CH
NG512 = WIN // 512
E, I_, GS = 16, 1536, 128
TI = 2 * I_
NIT = TI // CH
NIC = I_ // CH
DC = D // CH
GRP = 1024
NGRP = SPAN // GRP
UNITS = 17
EPS = 1e-5

SEMS = ["L", "G", "T", "PE", "DV", "AC", "OUT"]
ENGINES = ["sync", "gpsimd", "scalar", "vector", "tensor"]


class Builder:
    def __init__(self):
        self.nc = bass.Bass("TRN2", target_bir_lowering=False)
        self.steps = []
        self.cnt = {}
        self.semh = {}
        self._stack = contextlib.ExitStack()
        for s in SEMS:
            self.semh[s] = self._stack.enter_context(self.nc.semaphore(s))
            self.cnt[s] = 0

    def op(self, eng, fn, wait=(), inc=None, inc_n=1):
        w = tuple((s, v) for (s, v) in wait)
        if inc is not None:
            self.cnt[inc] += inc_n
            self.steps.append((eng, fn, w, (inc, inc_n)))
            return self.cnt[inc]
        self.steps.append((eng, fn, w, None))
        return None

    def dma(self, q, out, in_, wait=(), sem=None, transpose=False):
        sem = sem or {"sync": "L", "gpsimd": "G", "scalar": "T"}[q]

        def fn(e, out=out, in_=in_, t=transpose):
            return e.dma_start(out=out, in_=in_, transpose=t)

        return self.op(q, fn, wait=wait, inc=sem, inc_n=16)

    def bar(self):
        return [(s, self.cnt[s]) for s in SEMS]

    def fence(self, engines=ENGINES):
        w = self.bar()
        for eng in engines:
            self.op(eng, lambda e: None, wait=w)

    def emit(self):
        blk = self._stack.enter_context(self.nc.Block())
        per = {}
        for st in self.steps:
            per.setdefault(st[0], []).append(st)

        def run(eng_name, eng):
            for (_, fn, waits, inc) in per.get(eng_name, []):
                for (s, v) in waits:
                    eng.wait_ge(self.semh[s], v)
                inst = fn(eng)
                if inc is not None:
                    assert inst is not None
                    inst.then_inc(self.semh[inc[0]], inc[1])

        @blk.sync
        def _(e):
            run("sync", e)
            e.wait_ge(self.semh["OUT"], self.final_out)

        @blk.vector
        def _(e):
            run("vector", e)

        @blk.scalar
        def _(e):
            run("scalar", e)

        @blk.tensor
        def _(e):
            run("tensor", e)

        @blk.gpsimd
        def _(e):
            run("gpsimd", e)

        self._stack.close()


def host_consts():
    c = {}
    p = np.arange(128)
    c["shc"] = (4 * (p % 8)).astype(np.int32).reshape(128, 1)
    c["shc3"] = (4 * (p % 8) + 3).astype(np.int32).reshape(128, 1)
    c["iq"] = (p + 1).astype(np.float32).reshape(128, 1)
    c["ik"] = (127 - p).astype(np.float32).reshape(128, 1)
    s = p.reshape(128, 1)
    cc = p.reshape(1, 128)
    c["diffT"] = np.where(cc >= s, (cc - s).astype(np.float32),
                          np.float32(1e9)).astype(np.float32)
    import ml_dtypes
    c["ident"] = np.eye(128, dtype=np.float32).astype(ml_dtypes.bfloat16)
    return c


def build(debug=False, skip_kv=False):
    b = Builder()
    nc = b.nc

    dt_ = lambda name, shape, dt, kind: nc.dram_tensor(name, shape, dt, kind=kind).ap()
    d_x = dt_("x_in", [WIN, D], F32, "ExternalInput")
    d_Wq = dt_("Wq", [D, D], F32, "ExternalInput")
    d_Wk = dt_("Wk", [D, D], F32, "ExternalInput")
    d_Wv = dt_("Wv", [D, D], F32, "ExternalInput")
    d_Wo = dt_("Wo", [D, D], F32, "ExternalInput")
    d_Wg1 = dt_("Wg1", [D, HD], F32, "ExternalInput")
    d_Wg2 = dt_("Wg2", [HD, D], F32, "ExternalInput")
    d_ld = dt_("log_decay", [H], F32, "ExternalInput")
    d_nw = dt_("norm_w", [D], F32, "ExternalInput")
    d_gw = dt_("gate_w", [E, D], F32, "ExternalInput")
    d_gup = dt_("gup_pk", [E, D // 8, TI], I32, "ExternalInput")
    d_gus = dt_("gup_sc", [E, DC, TI], F32, "ExternalInput")
    d_dnp = dt_("dn_pk", [E, I_ // 8, D], I32, "ExternalInput")
    d_dns = dt_("dn_sc", [E, NIC, D], F32, "ExternalInput")
    d_sgup = dt_("sgup_pk", [D // 8, TI], I32, "ExternalInput")
    d_sgus = dt_("sgup_sc", [DC, TI], F32, "ExternalInput")
    d_sdnp = dt_("sdn_pk", [I_ // 8, D], I32, "ExternalInput")
    d_sdns = dt_("sdn_sc", [NIC, D], F32, "ExternalInput")
    d_shc = dt_("c_shc", [128, 1], I32, "ExternalInput")
    d_shc3 = dt_("c_shc3", [128, 1], I32, "ExternalInput")
    d_iq = dt_("c_iq", [128, 1], F32, "ExternalInput")
    d_ik = dt_("c_ik", [128, 1], F32, "ExternalInput")
    d_diffT = dt_("c_diffT", [128, 128], F32, "ExternalInput")
    d_ident = dt_("c_ident", [128, 128], BF16, "ExternalInput")
    d_out = dt_("out", [SPAN, D], F32, "ExternalOutput")

    s_xln = dt_("s_xln", [NG512, DC, 128, 512], BF16, "Internal")
    s_ktm = dt_("s_ktm", [NCH, 128, D], BF16, "Internal")
    s_v = dt_("s_v", [NCH, 128, D], BF16, "Internal")
    s_og = dt_("s_og", [OCH, 128, D], BF16, "Internal")
    s_x1 = dt_("s_x1", [OCH, 128, D], F32, "Internal")
    s_xf = dt_("s_xf", [DC, NGRP, 128, GRP], BF16, "Internal")
    s_wgu = dt_("s_wgu", [UNITS, NIT, 128, D], BF16, "Internal")
    s_wd = dt_("s_wd", [UNITS, 2, 128, NIC * 1024], BF16, "Internal")


    pb = [b._stack.enter_context(nc.psum_tensor(f"pb{i}", [128, 512], F32))
          for i in range(8)]

    ACOLS = 204 * 1024 // 4
    arena = b._stack.enter_context(nc.sbuf_tensor("arena", [128, ACOLS], I32))

    class Arena:
        def __init__(self):
            self.off = 0
            self.log = []

        def alloc(self, nbytes, align=64):
            o = (self.off + align - 1) // align * align
            self.off = o + nbytes
            assert self.off <= ACOLS * 4, f"SBUF arena overflow: {self.off}"
            self.log.append((o, nbytes))
            return o

        def mark(self):
            return self.off

        def reset(self, m):
            self.off = m

    A = Arena()

    def view(off, cols, dt):
        bytes_per = {F32: 4, BF16: 2, I32: 4, U16: 2}[dt]
        assert off % 4 == 0
        i32cols = (cols * bytes_per + 3) // 4
        ap = arena[:, off // 4: off // 4 + i32cols]
        if dt is I32:
            return ap
        return ap.bitcast(dt)[:, :cols]

    def f32t(cols):
        return view(A.alloc(cols * 4), cols, F32)

    def bft(cols):
        return view(A.alloc(cols * 2), cols, BF16)

    def i32t(cols):
        return view(A.alloc(cols * 4), cols, I32)

    def u16t(cols):
        return view(A.alloc(cols * 2), cols, U16)

    def vop(fn, wait=(), inc=None):
        return b.op("vector", fn, wait=wait, inc=inc)

    def aop(fn, wait=(), inc=None):
        return b.op("scalar", fn, wait=wait, inc=inc)

    def top(fn, wait=(), inc=None):
        return b.op("tensor", fn, wait=wait, inc=inc)

    cns = f32t(16)
    shc = i32t(1)
    shc3 = i32t(1)
    comb = f32t(OCH * E)
    normc = f32t(16)
    gwT = bft(DC * E)
    ident = bft(128)
    attn_base = A.mark()
    qdec = f32t(16)
    kdec = f32t(16)
    cdec = f32t(16)
    DhT = f32t(D)
    S = f32t(D)
    Sb = bft(D)
    phase_base = A.mark()

    ldrep = f32t(16)
    stage0 = f32t(2048)
    diffs = f32t(128)
    gwb = bft(2048)

    for i, val in enumerate([16128.0, 64.0, EPS, 1.0 / D, 128.0, 1.0, 1.0 / HD, -1.0]):
        vop(lambda e, o=cns[:, i:i + 1], v=val: e.memset(o, v))
    vop(lambda e: e.memset(S[:], 0.0))
    vset = vop(lambda e: e.memset(Sb[:], 0.0), inc="DV")
    b.dma("sync", shc[:], d_shc)
    b.dma("sync", shc3[:], d_shc3)
    b.dma("sync", ldrep[:], d_ld.unsqueeze(0).broadcast_to([128, H]))
    b.dma("sync", diffs[:], d_diffT)
    b.dma("sync", stage0[:, 0:1], d_iq)
    lc = b.dma("sync", stage0[:, 1:2], d_ik)
    aop(lambda e: e.activation(qdec[:], ldrep[:], AF.Exp, scale=stage0[:, 0:1]),
        wait=[("L", lc), ("DV", vset)], inc="AC")
    aop(lambda e: e.activation(kdec[:], ldrep[:], AF.Exp, scale=stage0[:, 1:2]), inc="AC")
    aop(lambda e: e.activation(cdec[:], ldrep[:], AF.Exp, scale=cns[:, 4:5]), inc="AC")
    for h in range(H):
        aop(lambda e, h=h: e.activation(DhT[:, h * 128:(h + 1) * 128], diffs[:], AF.Exp,
                                        scale=ldrep[:, h:h + 1]), inc="AC")
    for ec in range(DC):
        b.dma("sync", normc[:, ec:ec + 1], d_nw[ec * 128:(ec + 1) * 128].unsqueeze(1))
    lident = b.dma("sync", ident[:], d_ident)
    lc = b.dma("sync", stage0[:E, :], d_gw)
    gv = vop(lambda e: e.tensor_copy(gwb[:E, :], stage0[:E, :]), wait=[("L", lc)], inc="DV")
    for dc in range(DC):
        tp = pb[dc % 2][:].bitcast(BF16)
        pe = top(lambda e, o=tp[:, 0:E], i=gwb[:E, dc * 128:(dc + 1) * 128]:
                 e.transpose(o, i, ident[:E, :E]),
                 wait=[("DV", gv), ("L", lident), ("DV", b.cnt["DV"])], inc="PE")
        vop(lambda e, o=gwT[:, dc * E:(dc + 1) * E], i=tp[:, 0:E]:
            e.tensor_copy(o, i), wait=[("PE", pe)], inc="DV")

    b.fence()
    A.reset(phase_base)
    xfm = bft(DC * 512)
    Wk_sb = bft(DC * 1024)
    Wv_sb = bft(DC * 1024)
    wstage = [f32t(1024), f32t(1024)]
    xt = [f32t(D), f32t(D)]
    red = [f32t(8), f32t(8)]
    dump = f32t(D)
    xlb = [bft(D), bft(D)]
    kt_ev = [bft(1024), bft(1024)]
    vt_ev = [bft(1024), bft(1024)]
    assert A.mark() <= ACOLS * 4

    def layernorm(xin, xout, r, pre_wait=(), out_wait=()):
        v0 = vop(lambda e: e.reduce_sum(r[:, 0:1], xin, axis=AX.X), wait=pre_wait,
                 inc="DV")
        v1 = vop(lambda e: e.tensor_scalar(r[:, 1:2], r[:, 0:1], 1.0 / D, None, ALU.mult),
                 wait=[("DV", v0)], inc="DV")
        a1 = aop(lambda e: e.activation(dump[:], xin, AF.Square, accum_out=r[:, 2:3]),
                 wait=pre_wait, inc="AC")
        v2 = vop(lambda e: e.tensor_scalar(r[:, 3:4], r[:, 1:2], r[:, 1:2], None, ALU.mult),
                 wait=[("DV", v1), ("AC", a1)], inc="DV")
        v3 = vop(lambda e: e.scalar_tensor_tensor(r[:, 3:4], r[:, 2:3], 1.0 / D, r[:, 3:4],
                                                  ALU.mult, ALU.subtract),
                 wait=[("DV", v2)], inc="DV")
        a2 = aop(lambda e: e.activation(r[:, 4:5], r[:, 3:4], AF.Sqrt, bias=cns[:, 2:3]),
                 wait=[("DV", v3)], inc="AC")
        v4 = vop(lambda e: e.reciprocal(r[:, 5:6], r[:, 4:5]), wait=[("AC", a2)],
                 inc="DV")
        return vop(lambda e: e.tensor_scalar(xout, xin, r[:, 1:2], r[:, 5:6],
                                             ALU.subtract, ALU.mult),
                   wait=list(out_wait) + [("DV", v4)], inc="DV")

    def load_w_half(dsrc, dst_sb, half):
        last = 0
        w0 = [("PE", b.cnt["PE"])]
        for dc in range(DC):
            st = wstage[dc % 2]
            w = w0 if dc < 2 else [("DV", b.cnt["DV"])]
            lc = b.dma("sync", st[:],
                       dsrc[dc * 128:(dc + 1) * 128, half * 1024:(half + 1) * 1024],
                       wait=w)
            last = vop(lambda e, o=dst_sb[:, dc * 1024:(dc + 1) * 1024], i=st[:]:
                       e.tensor_copy(o, i), wait=[("L", lc)], inc="DV")
        return last

    for half in range(2):
        wkl = load_w_half(d_Wk, Wk_sb, half)
        wvl = load_w_half(d_Wv, Wv_sb, half)
        for grp in range(NG512):
            if half == 0:
                for c4 in range(4):
                    c = grp * 4 + c4
                    xb_ = xt[c4 % 2]
                    lc = b.dma("sync", xb_[:], d_x[c * 128:(c + 1) * 128, :],
                               wait=[("DV", b.cnt["DV"]), ("AC", b.cnt["AC"])])
                    dv = layernorm(xb_[:], xlb[c4 % 2][:], red[c4 % 2],
                                   pre_wait=[("L", lc)],
                                   out_wait=[("PE", b.cnt["PE"])])
                    for half8 in range(2):
                        tb = pb[6 + half8][:].bitcast(BF16)
                        for j in range(8):
                            dtl = half8 * 8 + j
                            pe = top(lambda e, o=tb[:, j * 128:(j + 1) * 128],
                                     i=xlb[c4 % 2][:, dtl * 128:(dtl + 1) * 128]:
                                     e.transpose(o, i, ident[:]),
                                     wait=([("DV", dv), ("DV", b.cnt["DV"])]
                                           if j == 0 else ()),
                                     inc="PE" if j == 7 else None)
                        vop(lambda e,
                            o=xfm[:].rearrange("p (a n) -> p a n", a=DC)
                            [:, half8 * 8:(half8 + 1) * 8, c4 * 128:(c4 + 1) * 128],
                            i=tb[:, 0:1024].rearrange("p (a n) -> p a n", a=8):
                            e.tensor_copy(o, i),
                            wait=[("PE", pe), ("G", b.cnt["G"])], inc="DV")
                if debug and grp == 5:
                    d_xfm = dt_("dbg_xfm", [128, DC * 512], BF16, "ExternalOutput")
                    b.dma("sync", d_xfm, xfm[:], wait=[("DV", b.cnt["DV"])])
                b.dma("gpsimd", s_xln[grp].rearrange("a p n -> p a n"),
                      xfm[:].rearrange("p (a n) -> p a n", a=DC),
                      wait=[("DV", b.cnt["DV"])])
            else:
                b.dma("sync", xfm[:].rearrange("p (a n) -> p a n", a=DC),
                      s_xln[grp].rearrange("a p n -> p a n"),
                      wait=[("PE", b.cnt["PE"]), ("G", b.cnt["G"])])
            xrL, xrDV, xrG = b.cnt["L"], b.cnt["DV"], b.cnt["G"]

            for c4 in range(4 if not skip_kv else 0):
                kb0, kb1 = pb[0], pb[1]
                vb0, vb1 = pb[4], pb[5]
                wmm = [("L", xrL), ("DV", xrDV), ("G", xrG),
                       ("DV", max(wkl, wvl)), ("AC", b.cnt["AC"])]
                for dc in range(DC):
                    lhs = xfm[:, dc * 512 + c4 * 128: dc * 512 + c4 * 128 + 128]
                    top(lambda e, p=kb0[:], l=lhs,
                        r=Wk_sb[:, dc * 1024: dc * 1024 + 512],
                        s=(dc == 0), t=(dc == DC - 1):
                        e.matmul(p, l, r, start=s, stop=t),
                        wait=wmm if dc == 0 else ())
                    top(lambda e, p=kb1[:], l=lhs,
                        r=Wk_sb[:, dc * 1024 + 512: dc * 1024 + 1024],
                        s=(dc == 0), t=(dc == DC - 1):
                        e.matmul(p, l, r, start=s, stop=t))
                    top(lambda e, p=vb0[:], l=lhs,
                        r=Wv_sb[:, dc * 1024: dc * 1024 + 512],
                        s=(dc == 0), t=(dc == DC - 1):
                        e.matmul(p, l, r, start=s, stop=t))
                    pe = top(lambda e, p=vb1[:], l=lhs,
                             r=Wv_sb[:, dc * 1024 + 512: dc * 1024 + 1024],
                             s=(dc == 0), t=(dc == DC - 1):
                             e.matmul(p, l, r, start=s, stop=t),
                             inc="PE" if dc == DC - 1 else None)
                kev = kt_ev[c4 % 2]
                vev = vt_ev[c4 % 2]
                aop(lambda e, o=kev[:, :512], i=kb0[:]: e.activation(o, i, AF.Silu),
                    wait=[("PE", pe), ("G", b.cnt["G"])])
                aop(lambda e, o=kev[:, 512:], i=kb1[:]: e.activation(o, i, AF.Silu))
                aop(lambda e, o=vev[:, :512], i=vb0[:]: e.activation(o, i, AF.Silu))
                ac = aop(lambda e, o=vev[:, 512:], i=vb1[:]: e.activation(o, i, AF.Silu),
                         inc="AC")
                b.dma("gpsimd", s_ktm[grp * 4 + c4][:, half * 1024:(half + 1) * 1024],
                      kev[:], wait=[("AC", ac)])
                b.dma("gpsimd", s_v[grp * 4 + c4][:, half * 1024:(half + 1) * 1024],
                      vev[:])

    b.fence()
    A.reset(phase_base)
    vt_c = [bft(D), bft(D)]
    kt_c = [bft(D), bft(D)]

    for c in range(HCH):
        p_ = c % 2
        lv = b.dma("sync", vt_c[p_][:], s_v[c], wait=[("PE", b.cnt["PE"])])
        lk = b.dma("sync", kt_c[p_][:], s_ktm[c])
        dvk = vop(lambda e, o=kt_c[p_][:], i=kt_c[p_][:],
                  k=kdec[:].unsqueeze(2).broadcast_to([128, H, 128]):
                  e.tensor_tensor(o.rearrange("p (a n) -> p a n", a=H),
                                  i.rearrange("p (a n) -> p a n", a=H), k, ALU.mult),
                  wait=[("L", lk)], inc="DV")
        for h in range(H):
            bank = pb[p_ * 4 + h % 4]
            pe = top(lambda e, p=bank[:, (h % 4) * 128:(h % 4 + 1) * 128],
                     l=kt_c[p_][:, h * 128:(h + 1) * 128],
                     r=vt_c[p_][:, h * 128:(h + 1) * 128]:
                     e.matmul(p, l, r, start=True, stop=True),
                     wait=[("DV", dvk), ("L", lv)] if h == 0 else (),
                     inc="PE" if h == H - 1 else None)
        for h in range(H):
            bank = pb[p_ * 4 + h % 4]
            vop(lambda e, o=S[:, h * 128:(h + 1) * 128], sc=cdec[:, h:h + 1],
                p=bank[:, (h % 4) * 128:(h % 4 + 1) * 128]:
                e.scalar_tensor_tensor(o, o, sc, p, ALU.mult, ALU.add),
                wait=[("PE", pe)] if h == 0 else (),
                inc="DV" if h == H - 1 else None)

    vop(lambda e: e.tensor_copy(Sb[:], S[:]), inc="DV")

    if debug:
        d_early = dt_("dbg_early", [NG512, DC, 128, 512], BF16, "ExternalOutput")
        b.dma("sync", d_early, s_xln, wait=b.bar(), sem="OUT")
    b.fence()
    A.reset(phase_base)
    Wq_sb = bft(DC * D)
    Wg1_sb = bft(D)
    Wg2_sb = bft(D)
    wstageB = [f32t(D)]
    xfmB = bft(DC * 512)
    q_sb = bft(H * 512)
    g1_sb = bft(512)
    kfB = [bft(D)]
    vtB = [bft(D)]
    ktB = [bft(D)]
    scm = bft(D)
    oi_sb = f32t(D)
    o_sb = f32t(D)
    sg_sb = f32t(D)
    ogn_sb = bft(D)
    redB = f32t(48)
    dumpB = oi_sb
    assert A.mark() <= ACOLS * 4

    for dc in range(DC):
        st = wstageB[0]
        lc = b.dma("sync", st[:], d_Wq[dc * 128:(dc + 1) * 128, :],
                   wait=[("DV", b.cnt["DV"])])
        vop(lambda e, o=Wq_sb[:, dc * D:(dc + 1) * D], i=st[:]: e.tensor_copy(o, i),
            wait=[("L", lc)], inc="DV")
    for dc in range(DC):
        st = wstageB[0][:, :128]
        lc = b.dma("sync", st, d_Wg1[dc * 128:(dc + 1) * 128, :],
                   wait=[("DV", b.cnt["DV"])])
        vop(lambda e, o=Wg1_sb[:, dc * 128:(dc + 1) * 128], i=st: e.tensor_copy(o, i),
            wait=[("L", lc)], inc="DV")
    lc = b.dma("sync", wstageB[0][:], d_Wg2, wait=[("DV", b.cnt["DV"])])
    vop(lambda e: e.tensor_copy(Wg2_sb[:], wstageB[0][:]), wait=[("L", lc)], inc="DV")

    for grp in range(4, NG512):
        lxf = b.dma("sync", xfmB[:].rearrange("p (a n) -> p a n", a=DC),
                    s_xln[grp].rearrange("a p n -> p a n"),
                    wait=[("PE", b.cnt["PE"])])
        for et in range(H):
            bank = pb[et % 2]
            wmm = [("L", lxf), ("DV", b.cnt["DV"]), ("AC", b.cnt["AC"])]
            for dc in range(DC):
                pe = top(lambda e, p=bank[:],
                         l=Wq_sb[:, dc * D + et * 128: dc * D + (et + 1) * 128],
                         r=xfmB[:, dc * 512:(dc + 1) * 512],
                         s=(dc == 0), t=(dc == DC - 1):
                         e.matmul(p, l, r, start=s, stop=t),
                         wait=wmm if dc == 0 else (),
                         inc="PE" if dc == DC - 1 else None)
            aop(lambda e, o=q_sb[:, et * 512:(et + 1) * 512], i=bank[:]:
                e.activation(o, i, AF.Silu), wait=[("PE", pe)], inc="AC")
        for dc in range(DC):
            pe = top(lambda e, p=pb[2][:], l=Wg1_sb[:, dc * 128:(dc + 1) * 128],
                     r=xfmB[:, dc * 512:(dc + 1) * 512],
                     s=(dc == 0), t=(dc == DC - 1):
                     e.matmul(p, l, r, start=s, stop=t),
                     wait=[("AC", b.cnt["AC"]), ("DV", b.cnt["DV"])] if dc == 0 else (),
                     inc="PE" if dc == DC - 1 else None)
        g1ac = aop(lambda e: e.activation(g1_sb[:], pb[2][:], AF.Identity),
                   wait=[("PE", pe)], inc="AC")

        for c4 in range(4):
            c = grp * 4 + c4
            oc = c - HCH
            p_ = 0
            lv = b.dma("sync", vtB[p_][:], s_v[c], wait=[("PE", b.cnt["PE"]),
                                                          ("DV", b.cnt["DV"])])
            lk = b.dma("sync", ktB[p_][:], s_ktm[c])
            for half8 in range(2):
                tb = pb[4 + half8][:].bitcast(BF16)
                for j in range(8):
                    h = half8 * 8 + j
                    pe = top(lambda e, o=tb[:, j * 128:(j + 1) * 128],
                             i=ktB[p_][:, h * 128:(h + 1) * 128]:
                             e.transpose(o, i, ident[:]),
                             wait=([("L", lk), ("DV", b.cnt["DV"])] if j == 0 else ()),
                             inc="PE" if j == 7 else None)
                vop(lambda e, o=kfB[p_][:, half8 * 1024:(half8 + 1) * 1024],
                    i=tb[:, 0:1024]: e.tensor_copy(o, i),
                    wait=[("PE", pe)], inc="DV")
            kf_dv = b.cnt["DV"]
            dvk = vop(lambda e, o=ktB[p_][:], i=ktB[p_][:],
                      k=kdec[:].unsqueeze(2).broadcast_to([128, H, 128]):
                      e.tensor_tensor(o.rearrange("p (a n) -> p a n", a=H),
                                      i.rearrange("p (a n) -> p a n", a=H), k, ALU.mult),
                      wait=[("L", lk)], inc="DV")

            for hg in range(4):
                qq = hg % 2
                bs = pb[2 + qq]
                bi = pb[4 + qq]
                bn = pb[6 + qq]
                bd = pb[0 + qq]
                for j in range(4):
                    h = hg * 4 + j
                    pe = top(lambda e, p=bs[:, j * 128:(j + 1) * 128],
                             l=kfB[p_][:, h * 128:(h + 1) * 128],
                             r=q_sb[:, h * 512 + c4 * 128: h * 512 + c4 * 128 + 128]:
                             e.matmul(p, l, r, start=True, stop=True),
                             wait=([("DV", kf_dv), ("AC", b.cnt["AC"]),
                                    ("DV", b.cnt["DV"])] if j == 0 else ()),
                             inc="PE" if j == 3 else None)
                dvm = vop(lambda e, o=scm[:, hg * 512:(hg + 1) * 512], i=bs[:],
                          m=DhT[:, hg * 512:(hg + 1) * 512]:
                          e.tensor_tensor(o, i, m, ALU.mult),
                          wait=[("PE", pe)], inc="DV")
                for j in range(4):
                    h = hg * 4 + j
                    top(lambda e, p=bi[:, j * 128:(j + 1) * 128],
                        l=scm[:, h * 128:(h + 1) * 128],
                        r=vtB[p_][:, h * 128:(h + 1) * 128]:
                        e.matmul(p, l, r, start=True, stop=True),
                        wait=[("DV", dvm), ("L", lv)] if j == 0 else ())
                    top(lambda e, p=bn[:, j * 128:(j + 1) * 128],
                        l=q_sb[:, h * 512 + c4 * 128: h * 512 + c4 * 128 + 128],
                        r=Sb[:, h * 128:(h + 1) * 128]:
                        e.matmul(p, l, r, start=True, stop=True))
                    pe = top(lambda e, p=bd[:, j * 128:(j + 1) * 128],
                             l=ktB[p_][:, h * 128:(h + 1) * 128],
                             r=vtB[p_][:, h * 128:(h + 1) * 128]:
                             e.matmul(p, l, r, start=True, stop=True),
                             inc="PE" if j == 3 else None)
                for j in range(4):
                    h = hg * 4 + j
                    aop(lambda e, o=oi_sb[:, h * 128:(h + 1) * 128],
                        i=bn[:, j * 128:(j + 1) * 128], sc=qdec[:, h:h + 1]:
                        e.activation(o, i, AF.Identity, scale=sc),
                        wait=[("PE", pe)] if j == 0 else (), inc="AC")
                acov = b.cnt["AC"]
                vop(lambda e, o=o_sb[:, hg * 512:(hg + 1) * 512], i=bi[:],
                    x=oi_sb[:, hg * 512:(hg + 1) * 512]:
                    e.tensor_tensor(o, i, x, ALU.add),
                    wait=[("AC", acov)])
                for j in range(4):
                    h = hg * 4 + j
                    vop(lambda e, o=S[:, h * 128:(h + 1) * 128], sc=cdec[:, h:h + 1],
                        p=bd[:, j * 128:(j + 1) * 128]:
                        e.scalar_tensor_tensor(o, o, sc, p, ALU.mult, ALU.add),
                        inc="DV" if j == 3 else None)
            vop(lambda e: e.tensor_copy(Sb[:], S[:]), inc="DV")
            for n4 in range(4):
                bk = pb[2 + ((c4 + n4) % 2)]
                pe = top(lambda e, p=bk[:],
                         l=g1_sb[:, c4 * 128:(c4 + 1) * 128],
                         r=Wg2_sb[:, n4 * 512:(n4 + 1) * 512]:
                         e.matmul(p, l, r, start=True, stop=True),
                         wait=[("AC", b.cnt["AC"]), ("DV", b.cnt["DV"])], inc="PE")
                aop(lambda e, o=sg_sb[:, n4 * 512:(n4 + 1) * 512], i=bk[:]:
                    e.activation(o, i, AF.Sigmoid), wait=[("PE", pe)], inc="AC")
            sgac = b.cnt["AC"]
            vop(lambda e: e.tensor_tensor(o_sb[:], o_sb[:], sg_sb[:], ALU.mult),
                wait=[("AC", sgac)])
            vop(lambda e: e.tensor_tensor(dumpB[:], o_sb[:], o_sb[:], ALU.mult))
            vr = vop(lambda e: e.reduce_sum(redB[:, 0:16].unsqueeze(2),
                                            dumpB[:].rearrange("p (a n) -> p a n", a=H),
                                            axis=AX.X), inc="DV")
            ars = aop(lambda e: e.activation(redB[:, 16:32], redB[:, 0:16], AF.Sqrt,
                                             scale=cns[:, 6:7], bias=cns[:, 2:3]),
                      wait=[("DV", vr)], inc="AC")
            vrc = vop(lambda e: e.reciprocal(redB[:, 32:48], redB[:, 16:32]),
                      wait=[("AC", ars)], inc="DV")
            dvn = vop(lambda e, o=ogn_sb[:]:
                      e.tensor_tensor(o.rearrange("p (a n) -> p a n", a=H),
                                      o_sb[:].rearrange("p (a n) -> p a n", a=H),
                                      redB[:, 32:48].unsqueeze(2).broadcast_to([128, H, 128]),
                                      ALU.mult),
                      wait=[("G", b.cnt["G"]), ("DV", vrc)], inc="DV")
            b.dma("gpsimd", s_og[oc], ogn_sb[:], wait=[("DV", dvn)])

    b.fence()
    A.reset(phase_base)
    Wo_sb = bft(DC * D)
    wstageC = [f32t(D), f32t(D)]
    ogf = [bft(D), bft(D)]
    ogtm = [bft(D), bft(D)]
    xc = [f32t(D), f32t(D)]
    x1_sb = f32t(D)
    xfb = bft(D)
    redC = [f32t(8), f32t(8)]
    dumpC = f32t(D)
    xff = bft(D)
    el = f32t(16)
    m8 = f32t(8)
    rsum = f32t(4)
    maskt = f32t(16)
    assert A.mark() <= ACOLS * 4

    dump = dumpC

    lastWo = 0
    for ec in range(DC):
        st = wstageC[ec % 2]
        lc = b.dma("sync", st[:], d_Wo[ec * 128:(ec + 1) * 128, :],
                   wait=[("DV", b.cnt["DV"])])
        lastWo = vop(lambda e, o=Wo_sb[:, ec * D:(ec + 1) * D], i=st[:],
                     sc=normc[:, ec:ec + 1]:
                     e.tensor_scalar(o, i, sc, None, ALU.mult),
                     wait=[("L", lc)], inc="DV")

    for oc in range(OCH):
        p_ = oc % 2
        lo = b.dma("sync", ogtm[p_][:], s_og[oc],
                   wait=[("PE", b.cnt["PE"]), ("G", b.cnt["G"])])
        for half8 in range(2):
            tb = pb[half8][:].bitcast(BF16)
            for j in range(8):
                ecb = half8 * 8 + j
                pe = top(lambda e, o=tb[:, j * 128:(j + 1) * 128],
                         i=ogtm[p_][:, ecb * 128:(ecb + 1) * 128]:
                         e.transpose(o, i, ident[:]),
                         wait=([("L", lo), ("DV", b.cnt["DV"])] if j == 0 else ()),
                         inc="PE" if j == 7 else None)
            vop(lambda e, o=ogf[p_][:, half8 * 1024:(half8 + 1) * 1024],
                i=tb[:, 0:1024]: e.tensor_copy(o, i),
                wait=[("PE", pe)], inc="DV")
        tog = b.cnt["DV"]
        lx = b.dma("sync", xc[p_][:], d_x[(HCH + oc) * 128:(HCH + oc + 1) * 128, :],
                   wait=[("DV", b.cnt["DV"])])
        for ec in range(DC):
            for n4 in range(4):
                pe = top(lambda e, p=pb[4 + n4][:],
                         l=ogf[p_][:, ec * 128:(ec + 1) * 128],
                         r=Wo_sb[:, ec * D + n4 * 512: ec * D + (n4 + 1) * 512],
                         s=(ec == 0), t=(ec == DC - 1):
                         e.matmul(p, l, r, start=s, stop=t),
                         wait=([("DV", tog), ("DV", b.cnt["DV"]),
                                ("AC", b.cnt["AC"])] if (ec == 0 and n4 == 0) else ()),
                         inc="PE" if (ec == DC - 1 and n4 == 3) else None)
        for n4 in range(4):
            vop(lambda e, o=x1_sb[:, n4 * 512:(n4 + 1) * 512], i=pb[4 + n4][:],
                x=xc[p_][:, n4 * 512:(n4 + 1) * 512]:
                e.tensor_tensor(o, i, x, ALU.add),
                wait=([("PE", pe), ("L", lx), ("G", b.cnt["G"]),
                       ("AC", b.cnt["AC"])] if n4 == 0 else ()),
                inc="DV" if n4 == 3 else None)
        x1dv = b.cnt["DV"]
        b.dma("gpsimd", s_x1[oc], x1_sb[:], wait=[("DV", x1dv)])
        dv2 = layernorm(x1_sb[:], xfb[:], redC[p_],
                        pre_wait=[("DV", x1dv)],
                        out_wait=[("PE", b.cnt["PE"])])
        for half8 in range(2):
            tb = pb[half8][:].bitcast(BF16)
            for j in range(8):
                dtl = half8 * 8 + j
                pe = top(lambda e, o=tb[:, j * 128:(j + 1) * 128],
                         i=xfb[:, dtl * 128:(dtl + 1) * 128]:
                         e.transpose(o, i, ident[:]),
                         wait=([("DV", dv2), ("DV", b.cnt["DV"]),
                                ("G", b.cnt["G"])] if j == 0 else ()),
                         inc="PE" if j == 7 else None)
            vop(lambda e, o=xff[:, half8 * 1024:(half8 + 1) * 1024],
                i=tb[:, 0:1024]: e.tensor_copy(o, i),
                wait=[("PE", pe)], inc="DV")
        txf = b.cnt["DV"]
        b.dma("gpsimd",
              s_xf[:, oc // 8, :, (oc % 8) * 128:(oc % 8) * 128 + 128]
              .rearrange("a p n -> p a n"),
              xff[:].rearrange("p (a n) -> p a n", a=DC),
              wait=[("DV", txf)])
        for dtl in range(DC):
            pe = top(lambda e, p=pb[3][:, 0:16], l=xff[:, dtl * 128:(dtl + 1) * 128],
                     r=gwT[:, dtl * E:(dtl + 1) * E],
                     s=(dtl == 0), t=(dtl == DC - 1):
                     e.matmul(p, l, r, start=s, stop=t),
                     wait=([("DV", txf), ("DV", b.cnt["DV"]),
                            ("AC", b.cnt["AC"])] if dtl == 0 else ()),
                     inc="PE" if dtl == DC - 1 else None)
        w1 = vop(lambda e: e.reduce_max(rsum[:, 0:1], pb[3][:, 0:16], axis=AX.X),
                 wait=[("PE", pe)], inc="DV")
        w2 = vop(lambda e: e.tensor_scalar(rsum[:, 1:2], rsum[:, 0:1], -1.0, None,
                                           ALU.mult), wait=[("DV", w1)], inc="DV")
        aex = aop(lambda e: e.activation(el[:], pb[3][:, 0:16], AF.Exp, bias=rsum[:, 1:2]),
                  wait=[("DV", w2)], inc="AC")
        w3 = vop(lambda e: e.max(m8[:], el[:]), wait=[("AC", aex)], inc="DV")
        w4 = vop(lambda e: e.reduce_sum(rsum[:, 2:3], m8[:, 0:4], axis=AX.X),
                 wait=[("DV", w3)], inc="DV")
        w5 = vop(lambda e: e.reciprocal(rsum[:, 3:4], rsum[:, 2:3]),
                 wait=[("DV", w4)], inc="DV")
        w6 = vop(lambda e: e.tensor_scalar(maskt[:], el[:], m8[:, 3:4], None, ALU.is_ge),
                 wait=[("DV", w5)], inc="DV")
        vop(lambda e, o=comb[:, oc * E:(oc + 1) * E]:
            e.scalar_tensor_tensor(o, el[:], rsum[:, 3:4], maskt[:], ALU.mult, ALU.mult),
            wait=[("DV", w6)], inc="DV")

    b.fence()
    A.reset(attn_base)
    DQ = 512
    pk_st = [i32t(DQ), i32t(DQ)]
    sc_st = f32t(TI)
    wt = [i32t(DQ), i32t(DQ)]
    srep_full = bft(TI)
    m7t = i32t(DQ)
    s1t = i32t(DQ)
    Bt = i32t(DQ)
    Xt = i32t(DQ)
    bitst = u16t(DQ)
    wdq = [bft(DQ), bft(DQ)]
    y_acc = f32t(8 * D)
    xfm_g = bft(DC * GRP)
    act_sb = bft(NIC * GRP)
    wgu_buf = [bft(DC * 128) for _ in range(2)]
    wd_sb = bft(NIC * GRP)
    sg2 = bft(GRP)
    assert A.mark() <= ACOLS * 4, A.mark()

    pp_state = [0]
    srep_dv = [0]

    def dequant_tile(dst_dram, words_src_rows, scale_slice, wait_pre):
        pp = pp_state[0]
        pp_state[0] ^= 1
        lw = b.dma("sync", pk_st[pp][:16, :], words_src_rows, wait=wait_pre)
        b.dma("gpsimd", wt[pp][:],
              pk_st[pp][:16, :].unsqueeze(1).broadcast_to([16, 8, DQ]),
              wait=[("L", lw)])
        ls = b.dma("gpsimd", wt[pp][:1, :1], wt[pp][:1, :1])
        return lw

    unit_ready_G = [None] * UNITS

    def dequant_unit(u):
        if u < E:
            gup_p, gup_s = d_gup[u], d_gus[u]
            dn_p, dn_s = d_dnp[u], d_dns[u]
        else:
            gup_p, gup_s = d_sgup, d_sgus
            dn_p, dn_s = d_sdnp, d_sdns
        lsg = b.dma("sync", sc_st[:DC, :], gup_s, wait=[("G", b.cnt["G"])])
        g_last = None
        for dc in range(DC):
            lsr = b.dma("gpsimd", srep_full[:],
                        sc_st[dc:dc + 1, :].unsqueeze(1).broadcast_to([1, 128, TI]),
                        wait=[("L", lsg), ("DV", srep_dv[0])])
            for sub in range(TI // DQ):
                pp = pp_state[0]
                pp_state[0] ^= 1
                lw = b.dma("sync", pk_st[pp][:16, :],
                           gup_p[dc * 16:(dc + 1) * 16, sub * DQ:(sub + 1) * DQ],
                           wait=[("DV", b.cnt["DV"])])
                lr = b.dma("gpsimd", wt[pp][:],
                           pk_st[pp][:16, :].unsqueeze(1).broadcast_to([16, 8, DQ]),
                           wait=[("L", lw)])
                vop(lambda e, w=wt[pp]: e.tensor_scalar(m7t[:], w[:], shc[:, 0:1], 7,
                                                        ALU.logical_shift_right,
                                                        ALU.bitwise_and),
                    wait=[("G", lr)])
                vop(lambda e, w=wt[pp]: e.tensor_scalar(s1t[:], w[:], shc3[:, 0:1], 1,
                                                        ALU.logical_shift_right,
                                                        ALU.bitwise_and))
                vop(lambda e: e.tensor_scalar(Bt[:], m7t[:], 64, 16128, ALU.mult, ALU.add))
                vop(lambda e: e.scalar_tensor_tensor(Xt[:], m7t[:], 16128, Bt[:],
                                                     ALU.mult, ALU.min))
                vop(lambda e: e.scalar_tensor_tensor(bitst[:], s1t[:], 32768, Xt[:],
                                                     ALU.mult, ALU.add))
                dv = vop(lambda e, o=wdq[pp], sc=srep_full[:, sub * DQ:(sub + 1) * DQ]:
                         e.tensor_tensor(o[:], bitst[:].bitcast(BF16), sc, ALU.mult),
                         inc="DV")
                srep_dv[0] = dv
                g_last = b.dma("gpsimd",
                               s_wgu[u, sub * 4:(sub + 1) * 4, :, dc * 128:(dc + 1) * 128]
                               .rearrange("a p n -> p a n"),
                               wdq[pp][:].rearrange("p (a n) -> p a n", a=4),
                               wait=[("DV", dv)])
        lsd = b.dma("sync", sc_st[:NIC, :D], dn_s, wait=[("G", b.cnt["G"])])
        for ic in range(NIC):
            lsr = b.dma("gpsimd", srep_full[:, :D],
                        sc_st[ic:ic + 1, :D].unsqueeze(1).broadcast_to([1, 128, D]),
                        wait=[("L", lsd), ("DV", srep_dv[0])])
            for sub in range(D // DQ):
                pp = pp_state[0]
                pp_state[0] ^= 1
                lw = b.dma("sync", pk_st[pp][:16, :],
                           dn_p[ic * 16:(ic + 1) * 16, sub * DQ:(sub + 1) * DQ],
                           wait=[("DV", b.cnt["DV"])])
                lr = b.dma("gpsimd", wt[pp][:],
                           pk_st[pp][:16, :].unsqueeze(1).broadcast_to([16, 8, DQ]),
                           wait=[("L", lw)])
                vop(lambda e, w=wt[pp]: e.tensor_scalar(m7t[:], w[:], shc[:, 0:1], 7,
                                                        ALU.logical_shift_right,
                                                        ALU.bitwise_and),
                    wait=[("G", lr)])
                vop(lambda e, w=wt[pp]: e.tensor_scalar(s1t[:], w[:], shc3[:, 0:1], 1,
                                                        ALU.logical_shift_right,
                                                        ALU.bitwise_and))
                vop(lambda e: e.tensor_scalar(Bt[:], m7t[:], 64, 16128, ALU.mult, ALU.add))
                vop(lambda e: e.scalar_tensor_tensor(Xt[:], m7t[:], 16128, Bt[:],
                                                     ALU.mult, ALU.min))
                vop(lambda e: e.scalar_tensor_tensor(bitst[:], s1t[:], 32768, Xt[:],
                                                     ALU.mult, ALU.add))
                dv = vop(lambda e, o=wdq[pp], sc=srep_full[:, sub * DQ:(sub + 1) * DQ]:
                         e.tensor_tensor(o[:], bitst[:].bitcast(BF16), sc, ALU.mult),
                         inc="DV")
                srep_dv[0] = dv
                g_last = b.dma("gpsimd",
                               s_wd[u, sub // 2][:, ic * 1024 + (sub % 2) * 512:
                                                 ic * 1024 + (sub % 2) * 512 + 512],
                               wdq[pp][:], wait=[("DV", dv)])
        unit_ready_G[u] = g_last

    dequant_unit(0)
    dequant_unit(1)

    bank_ac = [0] * 8
    bank_dv = [0] * 8
    slot_pe = [0, 0]
    wd_pe = [0]

    out_final = 0
    for grp in range(NGRP):
        lys = 0
        for tc in range(8):
            oc = grp * 8 + tc
            w = []
            if tc == 0:
                w = [("DV", b.cnt["DV"]), ("OUT", b.cnt["OUT"]), ("G", b.cnt["G"])]
            lys = b.dma("sync", y_acc[:, tc * D:(tc + 1) * D], s_x1[oc], wait=w)
        lxf = b.dma("sync", xfm_g[:].rearrange("p (a n) -> p a n", a=DC),
                    s_xf[:, grp].rearrange("a p n -> p a n"),
                    wait=[("PE", b.cnt["PE"]), ("G", b.cnt["G"])])
        y_ready = lys

        lg_slot = [None, None]
        for u in range(UNITS):
            uw = [("G", unit_ready_G[u])]

            use_order = []
            for ip in range(NIC):
                use_order.append(ip)
                use_order.append(ip + NIC)
            for idx, it in enumerate(use_order):
                slot = idx % 2
                if lg_slot[slot] is None or True:
                    lg = b.dma("sync",
                               wgu_buf[slot][:],
                               s_wgu[u, it],
                               wait=uw + [("PE", slot_pe[slot])])
                ip = it % NIC
                half = it // NIC
                qq = ip % 2
                for th in range(2):
                    bank = 4 * qq + 2 * half + th
                    bk = pb[bank]
                    for dc in range(DC):
                        pe = top(lambda e, p=bk[:],
                                 l=wgu_buf[slot][:, dc * 128:(dc + 1) * 128],
                                 r=xfm_g[:, dc * GRP + th * 512: dc * GRP + th * 512 + 512],
                                 s=(dc == 0), t=(dc == DC - 1):
                                 e.matmul(p, l, r, start=s, stop=t),
                                 wait=([("L", lg), ("L", lxf),
                                        ("AC", bank_ac[bank]),
                                        ("DV", bank_dv[bank])] if dc == 0 else ()),
                                 inc="PE" if dc == DC - 1 else None)
                    slot_pe[slot] = pe
                if half == 0:
                    for th in range(2):
                        bank = 4 * qq + th
                        sdone = aop(lambda e, o=sg2[:, th * 512:(th + 1) * 512],
                                    i=pb[bank][:]: e.activation(o, i, AF.Silu),
                                    wait=[("PE", slot_pe[slot])] if th == 0 else (),
                                    inc="AC")
                        bank_ac[bank] = sdone
                else:
                    for th in range(2):
                        bank = 4 * qq + 2 + th
                        dvv = vop(lambda e,
                                  o=act_sb[:, ip * GRP + th * 512: ip * GRP + th * 512 + 512],
                                  i=pb[bank][:], x=sg2[:, th * 512:(th + 1) * 512]:
                                  e.tensor_tensor(o, i, x, ALU.mult),
                                  wait=([("PE", slot_pe[slot]),
                                         ("AC", bank_ac[4 * qq + 1])] if th == 0 else ()),
                                  inc="DV")
                        bank_dv[bank] = dvv
            act_done = b.cnt["DV"]
            for hh in range(2):
                lwd = b.dma("sync", wd_sb[:], s_wd[u, hh],
                            wait=uw + [("PE", wd_pe[0])])
                for tc in range(8):
                    qq = tc % 2
                    for ic in range(NIC):
                        for hc in range(2):
                            bank = 4 * qq + hc
                            pe = top(lambda e, p=pb[bank][:],
                                     l=act_sb[:, ic * GRP + tc * 128: ic * GRP + tc * 128 + 128],
                                     r=wd_sb[:, ic * 1024 + hc * 512:
                                             ic * 1024 + (hc + 1) * 512],
                                     s=(ic == 0), t=(ic == NIC - 1):
                                     e.matmul(p, l, r, start=s, stop=t),
                                     wait=([("DV", act_done), ("L", lwd), ("L", y_ready),
                                            ("AC", bank_ac[4 * qq]),
                                            ("AC", bank_ac[4 * qq + 1]),
                                            ("DV", bank_dv[4 * qq + hc]),
                                            ("DV", bank_dv[4 * qq + 2]),
                                            ("DV", bank_dv[4 * qq + 3])]
                                           if (ic == 0 and hc == 0) else ()),
                                     inc="PE" if (ic == NIC - 1 and hc == 1) else None)
                    oc = grp * 8 + tc
                    sc_ap = cns[:, 5:6] if u == E else comb[:, oc * E + u: oc * E + u + 1]
                    for hc in range(2):
                        bank = 4 * qq + hc
                        dvv = vop(lambda e,
                                  o=y_acc[:, tc * D + hh * 1024 + hc * 512:
                                          tc * D + hh * 1024 + (hc + 1) * 512],
                                  p=pb[bank][:], sc=sc_ap:
                                  e.scalar_tensor_tensor(o, p, sc, o, ALU.mult, ALU.add),
                                  wait=[("PE", pe)] if hc == 0 else (),
                                  inc="DV")
                        bank_dv[bank] = dvv
                wd_pe[0] = b.cnt["PE"]
            if grp == 0 and u + 2 < UNITS:
                dequant_unit(u + 2)
        ydone = b.cnt["DV"]
        for tc in range(8):
            out_final = b.dma("sync",
                              d_out[(grp * 8 + tc) * 128:(grp * 8 + tc + 1) * 128, :],
                              y_acc[:, tc * D:(tc + 1) * D],
                              wait=[("DV", ydone)] if tc == 0 else (), sem="OUT")

    if debug:
        def dbg_dump(name, src_ap, shape, dt):
            dd = dt_(name, shape, dt, "ExternalOutput")
            return b.dma("sync", dd, src_ap, wait=[("OUT", b.cnt["OUT"]),
                                                   ("G", b.cnt["G"]),
                                                   ("DV", b.cnt["DV"])], sem="OUT")
        dbg_dump("dbg_xln", s_xln, [NG512, DC, 128, 512], BF16)
        dbg_dump("dbg_k", s_ktm, [NCH, 128, D], BF16)
        dbg_dump("dbg_v", s_v, [NCH, 128, D], BF16)
        dbg_dump("dbg_og", s_og, [OCH, 128, D], BF16)
        dbg_dump("dbg_x1", s_x1, [OCH, 128, D], F32)
        dbg_dump("dbg_xf", s_xf, [DC, NGRP, 128, GRP], BF16)
        dbg_dump("dbg_wgu", s_wgu[0], [NIT, DC, 128, 128], BF16)
        dbg_dump("dbg_wd", s_wd[0], [NIC, 128, D], BF16)
        dbg_dump("dbg_S", S[:], [128, D], F32)
        out_final = dbg_dump("dbg_comb", comb[:], [128, OCH * E], F32)

    b.final_out = out_final
    b.arena_log = A.log
    b.emit()
    return b


_BUILT = None


def _get_built():
    global _BUILT
    if _BUILT is None:
        _BUILT = build()
    return _BUILT


def kernel(**inputs):
    hc = host_consts()
    x = np.ascontiguousarray(np.asarray(inputs["L0_attn_x"], dtype=np.float32))
    base = {
        "Wq": np.ascontiguousarray(inputs["L0_attn_Wq"], dtype=np.float32),
        "Wk": np.ascontiguousarray(inputs["L0_attn_Wk"], dtype=np.float32),
        "Wv": np.ascontiguousarray(inputs["L0_attn_Wv"], dtype=np.float32),
        "Wo": np.ascontiguousarray(inputs["L0_attn_Wo"], dtype=np.float32),
        "Wg1": np.ascontiguousarray(inputs["L0_attn_Wg1"], dtype=np.float32),
        "Wg2": np.ascontiguousarray(inputs["L0_attn_Wg2"], dtype=np.float32),
        "log_decay": np.ascontiguousarray(inputs["L0_attn_log_decay"], dtype=np.float32),
        "norm_w": np.ascontiguousarray(inputs["L0_attn_norm_w"], dtype=np.float32),
        "gate_w": np.ascontiguousarray(inputs["L0_ffn_gate_w"], dtype=np.float32),
        "gup_pk": np.ascontiguousarray(inputs["L0_ffn_gate_up_packed"], dtype=np.int32),
        "gup_sc": np.ascontiguousarray(inputs["L0_ffn_gate_up_scales"], dtype=np.float32),
        "dn_pk": np.ascontiguousarray(inputs["L0_ffn_down_packed"], dtype=np.int32),
        "dn_sc": np.ascontiguousarray(inputs["L0_ffn_down_scales"], dtype=np.float32),
        "sgup_pk": np.ascontiguousarray(inputs["L0_ffn_shared_gate_up_packed"], dtype=np.int32),
        "sgup_sc": np.ascontiguousarray(inputs["L0_ffn_shared_gate_up_scales"], dtype=np.float32),
        "sdn_pk": np.ascontiguousarray(inputs["L0_ffn_shared_down_packed"], dtype=np.int32),
        "sdn_sc": np.ascontiguousarray(inputs["L0_ffn_shared_down_scales"], dtype=np.float32),
        "c_shc": hc["shc"], "c_shc3": hc["shc3"], "c_iq": hc["iq"], "c_ik": hc["ik"],
        "c_diffT": hc["diffT"], "c_ident": hc["ident"],
    }
    in_maps = []
    for core in range(NCORES):
        bi, sp = core // SPANS, core % SPANS
        s0 = sp * SPAN
        own = x[bi, s0:s0 + SPAN]
        if sp == 0:
            halo = np.zeros((HALO, D), np.float32)
        else:
            halo = x[bi, s0 - HALO:s0]
        m = dict(base)
        m["x_in"] = np.ascontiguousarray(np.concatenate([halo, own], axis=0))
        in_maps.append(m)
    bd = _get_built()
    res = run_bass_kernel_spmd(bd.nc, in_maps, core_ids=list(range(NCORES)))
    outs = [np.asarray(res.results[c]["out"], dtype=np.float32) for c in range(NCORES)]
    full = np.stack(outs, 0).reshape(B, SPANS * SPAN, D)
    return full
